# version 07
# baseline (speedup 1.0000x reference)
import numpy as np
from contextlib import ExitStack

import concourse.bass as bass
import concourse.mybir as mybir
from concourse.bass_utils import run_bass_kernel_spmd

F32 = mybir.dt.float32
BF16 = mybir.dt.bfloat16
I32 = mybir.dt.int32
AF = mybir.ActivationFunctionType
ALU = mybir.AluOpType
AX = mybir.AxisListType

T = 2048
C = 1024
H = 16
HD = 64
LAT = 32
NS = 16
NT = T // 128
NCT = C // 128
SCALE = LAT ** -0.5
EPS_LN = 1e-5
QMAX = 127.0
EPS_Q = 1e-8
CONV_PAD = 16
BAND = T + CONV_PAD

ENGS = ("sync", "scalar", "vector", "tensor", "gpsimd")
DMA_ENGS = ("sync", "scalar", "gpsimd", "vector", "tensor")


class Prog:

    def __init__(self, nc, es):
        self.nc = nc
        self.ops = {e: [] for e in ENGS}
        self.csem = {e: es.enter_context(nc.semaphore(f"c_{e}")) for e in ENGS}
        self.dsem = {e: es.enter_context(nc.semaphore(f"d_{e}")) for e in DMA_ENGS}
        self.ccnt = {e: 0 for e in ENGS}
        self.dcnt = {e: 0 for e in DMA_ENGS}
        self.waited = {e: {} for e in ENGS}

    def step(self, eng, fn, deps=(), dma=False, n_dma=1, flush=False):
        deps = tuple(d for d in deps if d is not None)
        csem = self.csem[eng]
        sem = self.dsem[eng] if dma else csem
        prev_d = None
        if dma:
            if eng == "gpsimd" and self.dcnt[eng] > 0:
                prev_d = (sem, self.dcnt[eng])
            self.dcnt[eng] += 16 * n_dma
            tok = (sem, self.dcnt[eng])
        else:
            self.ccnt[eng] += 1
            tok = (sem, self.ccnt[eng])

        waited = self.waited[eng]

        def run(e, deps=deps, fn=fn, sem=sem, dma=dma, csem=csem, tok=tok,
                flush=flush, waited=waited, prev_d=prev_d):
            if prev_d is not None:
                k = id(prev_d[0])
                if waited.get(k, -1) < prev_d[1]:
                    waited[k] = prev_d[1]
                    e.wait_ge(prev_d[0], prev_d[1])
            for (s, v) in deps:
                if s is csem:
                    continue
                k = id(s)
                if waited.get(k, -1) >= v:
                    continue
                waited[k] = v
                e.wait_ge(s, v)
            if dma and waited.get(id(sem), -1) < tok[1]:
                pass
            r = fn(e)
            rs = r if isinstance(r, (list, tuple)) else [r]
            rs = [i for i in rs if i is not None]
            if dma:
                for ins in rs:
                    ins.then_inc(sem, 16)
            else:
                rs[-1].then_inc(sem, 1)
                if flush:
                    waited[id(sem)] = tok[1]
                    e.wait_ge(sem, tok[1])

        self.ops[eng].append(run)
        return tok

    def dma(self, eng, fn, deps=(), n=1):
        return self.step(eng, fn, deps=deps, dma=True, n_dma=n)


def build_nc(debug=()):
    nc = bass.Bass("TRN2", target_bir_lowering=False)
    dbg = set(debug)

    def din(name, shape, dt=F32):
        return nc.dram_tensor(name, shape, dt, kind="ExternalInput").ap()

    def dout(name, shape, dt=F32):
        return nc.dram_tensor(name, shape, dt, kind="ExternalOutput").ap()

    def dint(name, shape, dt):
        return nc.dram_tensor(name, shape, dt, kind="Internal").ap()

    x_in = din("x", [T, C])
    cattnT = din("c_attnT", [C, 3 * C])
    cattnb_pc = din("c_attnb_pc", [128, 24])
    q2l_blk = din("c_q2l_blk", [128, 64])
    kv2l_blk = din("c_kv2l_blk", [128, 64])
    q2l_b = din("L0_attn_a_q2l_b", [LAT])
    q2l_b_swap = din("c_q2l_b_swap", [LAT])
    kv2l_b = din("L0_attn_a_kv2l_b", [LAT])
    kv2l_b_swap = din("c_kv2l_b_swap", [LAT])
    kv2l_w = din("L0_attn_a_kv2l_w", [LAT, HD])
    l2oT = din("c_l2oT", [LAT, HD])
    l2o_b = din("L0_attn_a_l2o_b", [HD])
    wqk_blkT = din("c_wqk_blkT", [128, 128])
    wqk_b4 = din("c_wqk_b4", [128])
    cprojT = din("c_cprojT", [C, C])
    cproj_b = din("L0_attn_a_cproj_b", [C])
    ffn_w = din("L0_ffn_weight", [C, C])
    ffn_wT = din("c_ffn_wT", [C, C])
    ffn_b = din("L0_ffn_bias", [C])
    conv_w = din("c_convw", [128, NCT * 4])
    convb_pc = din("c_convb_pc", [128, NCT])
    xprojT_pad = din("c_xprojT_pad", [C, 65])
    dtw_pc = din("c_dtw_pc", [128, NCT])
    dtb_pc = din("c_dtb_pc", [128, NCT])
    a_log = din("L0_attn_b_a_log", [C, NS])
    dp_pc = din("c_dp_pc", [128, NCT])
    cos_t = din("c_cos", [T, LAT])
    ss_t = din("c_ss", [T, LAT])
    trimask = din("c_trimask", [128, 128])
    onehot16_in = din("c_onehot16", [16, 16 * 64])

    out = dout("out", [T, C])

    s_xn1 = dint("s_xn1", [T, C], BF16)
    s_qr = dint("s_qr", [T, 128], BF16)
    s_kr = dint("s_kr", [T, 128], BF16)
    s_xres = dint("s_xres", [T, C], F32)
    s_xq = dint("s_xq", [T, C], BF16)
    s_xn3 = dint("s_xn3", [T, C], BF16)
    s_bc = dint("s_bc", [32, T], BF16)
    s_bcf = dint("s_bcf", [16, T], F32)
    s_xc = dint("s_xc", [C, T], BF16)
    s_ws = dint("s_ws", [C], F32)
    s_yhi = dint("s_yhi", [C, T], BF16)
    s_den = dint("s_den", [16, 512], F32)
    s_ylo = dint("s_ylo", [C, T], BF16)

    dbg_t = {}
    for name, shape, dt in (
        ("xn1", [T, C], F32), ("yT", [128, NCT * T], F32), ("x1", [T, C], F32),
        ("x2", [T, C], F32), ("xc", [128, NCT * T], F32), ("proj", [128, T], F32),
        ("delta", [128, 2 * T], F32), ("x3", [T, C], F32),
        ("yhi", [C, T], F32), ("ylo", [C, T], F32), ("ussm", [128, 2 * T], F32),
    ):
        if name in dbg:
            dbg_t[name] = dout("dbg_" + name, shape, dt)

    es = ExitStack()

    def sb(name, shape, dt):
        return es.enter_context(nc.sbuf_tensor(name, shape, dt))

    p = Prog(nc, es)

    cpool = sb("cpool", [128, 8], F32)
    _cvals = [EPS_LN, 1.0, -0.5, -1.0, 0.0, 2.0, 0.5, -2.0]
    for _j, _v in enumerate(_cvals):
        nc.const_aps.register(F32, _v, cpool[:, _j:_j+1])
    t_cpool = p.step("vector", lambda e: [
        e.memset(cpool[:, _j:_j+1], _v) for _j, _v in enumerate(_cvals)])
    ones_f = sb("ones_f", [128, 128], F32)
    cosT = sb("cosT", [128, NT * LAT], BF16)
    ssT = sb("ssT", [128, NT * LAT], BF16)
    tri = sb("tri", [128, 128], BF16)
    cattnb_c = sb("cattnb_c", [128, 24], F32)
    q2lblk_s = sb("q2lblk_s", [128, 64], BF16)
    kv2lblk_s = sb("kv2lblk_s", [128, 64], BF16)
    wqkblk_s = sb("wqkblk_s", [128, 128], BF16)
    wqkb_c = sb("wqkb_c", [128, 1], F32)
    vob_c = sb("vob_c", [128, 1], F32)
    wvoblk = sb("wvoblk", [128, 128], BF16)
    ropebq = sb("ropebq", [128, NT * LAT], F32)
    ropebk = sb("ropebk", [128, NT * LAT], F32)
    cprojbB = sb("cprojbB", [128, C], F32)
    wqT_s = sb("wqT_s", [128, NCT * C], BF16)
    wsB = sb("wsB", [128, C], F32)
    ffnbB = sb("ffnbB", [128, C], F32)
    kvnat = sb("kvnat", [128, HD], BF16)
    l2oT_s = sb("l2oT_s", [128, HD], BF16)
    kvb_c = sb("kvb_c", [128, 1], F32)
    kvb_bf = sb("kvb_bf", [128, 1], BF16)
    l2ob_c = sb("l2ob_c", [128, 1], F32)
    ws_c = sb("ws_c", [128, NCT], F32)
    bq = sb("bq", [128, LAT], F32)
    bqs = sb("bqs", [128, LAT], F32)
    bk = sb("bk", [128, LAT], F32)
    bks = sb("bks", [128, LAT], F32)

    xtile2 = [sb(f"xtile{i}", [128, C], F32) for i in range(2)]
    sqscr2 = [sb(f"sqscr{i}", [128, C], F32) for i in range(2)]
    rowsc = [sb(f"rowsc{i}", [128, 8], F32) for i in range(2)]
    xtile = xtile2[0]
    sqscr = sqscr2[0]

    def ln_tile(tk_ld, out_fn, extra_deps=(), par=0):
        xt = xtile2[par]
        sq = sqscr2[par]
        rc = rowsc[par]
        s1 = rc[:, 0:1]; s2 = rc[:, 1:2]; negm = rc[:, 2:3]
        vr = rc[:, 3:4]; msq = rc[:, 4:5]
        tk_s1 = p.step("vector", lambda e: e.reduce_sum(out=s1, in_=xt[:], axis=AX.X),
                       deps=[tk_ld] + list(extra_deps))
        tk_s2 = p.step("scalar", lambda e: e.activation(out=sq[:], in_=xt[:],
                                                        func=AF.Square, accum_out=s2),
                       deps=[tk_ld] + list(extra_deps))
        tk_pre = p.step("scalar", lambda e: [
            e.activation(out=negm, in_=s1, func=AF.Identity, scale=-1.0 / C),
            e.activation(out=msq, in_=s1, func=AF.Square, scale=1.0 / C),
        ], deps=[tk_s1])
        tk_rows = p.step("vector", lambda e: e.scalar_tensor_tensor(
            out=vr, in0=s2, scalar=1.0 / C, in1=msq,
            op0=ALU.mult, op1=ALU.subtract), deps=[tk_pre, tk_s2], flush=True)
        tk_ln = p.step("scalar", lambda e: e.activation(
            out=vr, in_=vr, func=AF.Ln, bias=EPS_LN),
            deps=[tk_rows, t_cpool], flush=True)
        tk_rstd = p.step("scalar", lambda e: e.activation(
            out=vr, in_=vr, func=AF.Exp, scale=-0.5), deps=[tk_ln])
        return p.step("vector", lambda e: out_fn(e, xt, negm, vr), deps=[tk_rstd])

    es_p = ExitStack()
    def sbp(name, shape, dt):
        return es_p.enter_context(nc.sbuf_tensor(name, shape, dt))
    cosTf = sbp("cosTf", [128, NT * LAT], F32)
    ssTf = sbp("ssTf", [128, NT * LAT], F32)
    wqkblk_f = sbp("wqkblk_f", [128, 128], F32)
    ropetmp = sbp("ropetmp", [128, NT * LAT], F32)
    wrsB = sbp("wrsB", [128, C], F32)
    wnat = sbp("wnat", [128, NCT * C], F32)
    wTf = sbp("wTf", [128, NCT * C], F32)
    wq_i = sbp("wq_i", [128, NCT * C], I32)

    tok_cb = p.dma("sync", lambda e: e.dma_start(out=cattnb_c[:], in_=cattnb_pc[:, :]))

    t_consts = p.dma("gpsimd", lambda e: [
        e.dma_start(out=cosT[:], in_=cos_t.rearrange("(tt p) l -> p tt l", p=128)),
        e.dma_start(out=ssT[:], in_=ss_t.rearrange("(tt p) l -> p tt l", p=128)),
        e.dma_start(out=tri[:], in_=trimask[:, :]),
        e.dma_start(out=q2lblk_s[:], in_=q2l_blk[:, :]),
        e.dma_start(out=kv2lblk_s[:], in_=kv2l_blk[:, :]),
    ], n=5)

    t_consts2 = p.dma("gpsimd", lambda e: [
        e.dma_start(out=cprojbB[:], in_=cproj_b[None, :].broadcast_to((128, C))),
        e.dma_start(out=ffnbB[:], in_=ffn_b[None, :].broadcast_to((128, C))),
        e.dma_start(out=kvnat[0:LAT, :], in_=kv2l_w[:, :]),
        e.dma_start(out=l2oT_s[0:LAT, :], in_=l2oT[:, :]),
        e.dma_start(out=kvb_c[0:LAT, 0:1], in_=kv2l_b[:, None]),
        e.dma_start(out=l2ob_c[0:HD, 0:1], in_=l2o_b[:, None]),
    ], n=6)

    t_constsf = p.dma("sync", lambda e: [
        e.dma_start(out=cosTf[:], in_=cos_t.rearrange("(tt p) l -> p tt l", p=128)),
        e.dma_start(out=ssTf[:], in_=ss_t.rearrange("(tt p) l -> p tt l", p=128)),
        e.dma_start(out=wqkblk_f[:], in_=wqk_blkT[:, :]),
        e.dma_start(out=rowsc[0][:, 7:8], in_=wqk_b4[0:128, None]),
    ], n=4)

    t_ones = p.step("vector", lambda e: e.memset(ones_f[:], 1.0))

    t_wqk = p.step("vector", lambda e: [
        e.tensor_scalar(out=wqkblk_s[:], in0=wqkblk_f[:], scalar1=SCALE, scalar2=None, op0=ALU.mult),
        e.tensor_scalar(out=wqkb_c[:], in0=rowsc[0][:, 7:8], scalar1=SCALE, scalar2=None, op0=ALU.mult),
    ], deps=[t_constsf])

    t_kvbbf = p.step("vector", lambda e: e.tensor_copy(out=kvb_bf[0:LAT, :], in_=kvb_c[0:LAT, :]),
                     deps=[t_consts2])

    es_pp = ExitStack()
    ps_vo_p = es_pp.enter_context(nc.psum_tensor("ps_vo_p", [128, 128], F32))
    t_mmvo = p.step("tensor", lambda e: [
        e.matmul(ps_vo_p[0:HD, 0:HD], lhsT=kvnat[0:LAT, :], rhs=l2oT_s[0:LAT, :],
                 start=True, stop=True),
        e.matmul(ps_vo_p[0:HD, 64:65], lhsT=l2oT_s[0:LAT, :], rhs=kvb_bf[0:LAT, 0:1],
                 start=True, stop=True),
    ], deps=[t_consts2, t_kvbbf])

    t_wvo0 = p.step("vector", lambda e: e.memset(wvoblk[:], 0.0),
                    deps=[t_mmvo], flush=True)
    t_wvo = p.step("vector", lambda e: [
        e.tensor_copy(out=wvoblk[0:HD, 0:HD], in_=ps_vo_p[0:HD, 0:HD]),
        e.tensor_copy(out=wvoblk[64:128, 64:128], in_=ps_vo_p[0:HD, 0:HD]),
    ], deps=[t_wvo0])

    t_vob0 = p.step("vector", lambda e: e.tensor_tensor(
        out=vob_c[0:HD, 0:1], in0=ps_vo_p[0:HD, 64:65],
        in1=l2ob_c[0:HD, 0:1], op=ALU.add), deps=[t_mmvo, t_consts2], flush=True)
    t_vob = p.step("vector", lambda e: e.tensor_copy(
        out=vob_c[64:128, 0:1], in_=vob_c[0:HD, 0:1]), deps=[t_vob0], flush=True)
    es_pp.close()

    t_rbld = p.dma("gpsimd", lambda e: [
        e.dma_start(out=bq[:], in_=q2l_b[None, :].broadcast_to((128, LAT))),
        e.dma_start(out=bqs[:], in_=q2l_b_swap[None, :].broadcast_to((128, LAT))),
        e.dma_start(out=bk[:], in_=kv2l_b[None, :].broadcast_to((128, LAT))),
        e.dma_start(out=bks[:], in_=kv2l_b_swap[None, :].broadcast_to((128, LAT))),
    ], n=4)

    def mk_ropeb(e):
        insts = []
        for bb, bbs, dst in ((bq, bqs, ropebq), (bk, bks, ropebk)):
            r3 = lambda t_: t_[:, None, :].broadcast_to((128, NT, LAT))
            c3 = cosTf.rearrange("p (tt l) -> p tt l", tt=NT)
            s3 = ssTf.rearrange("p (tt l) -> p tt l", tt=NT)
            d3 = dst.rearrange("p (tt l) -> p tt l", tt=NT)
            t3 = ropetmp.rearrange("p (tt l) -> p tt l", tt=NT)
            insts.append(e.tensor_tensor(out=d3[:], in0=c3[:], in1=r3(bb), op=ALU.mult))
            insts.append(e.tensor_tensor(out=t3[:], in0=s3[:], in1=r3(bbs), op=ALU.mult))
            insts.append(e.tensor_tensor(out=dst[:], in0=dst[:], in1=ropetmp[:], op=ALU.add))
        return insts
    t_ropeb = p.step("vector", mk_ropeb, deps=[t_rbld, t_constsf])

    t_ffnld = p.dma("sync", lambda e: [
        e.dma_start(out=wnat[:], in_=ffn_w.rearrange("(ot p) c -> p ot c", p=128)),
        e.dma_start(out=wTf[:], in_=ffn_wT.rearrange("(ct p) o -> p ct o", p=128)),
    ], n=2)

    t_wamax0 = p.step("vector", lambda e: [
        e.reduce_max(out=ws_c[:, ot:ot+1], in_=wnat[:, ot*C:(ot+1)*C],
                     axis=AX.X, apply_absolute_value=True) for ot in range(NCT)],
        deps=[t_ffnld], flush=True)
    t_wamax = p.step("vector", lambda e: e.tensor_scalar(
        out=ws_c[:], in0=ws_c[:], scalar1=EPS_Q, scalar2=1.0 / QMAX,
        op0=ALU.max, op1=ALU.mult), deps=[t_wamax0], flush=True)

    t_wssave = p.dma("gpsimd", lambda e: [
        e.dma_start(out=s_ws[ot*128:(ot+1)*128, None], in_=ws_c[:, ot:ot+1])
        for ot in range(NCT)], deps=[t_wamax], n=NCT)
    t_wsB = p.dma("gpsimd", lambda e: e.dma_start(
        out=wsB[:], in_=s_ws[None, :].broadcast_to((128, C))), deps=[t_wssave])
    t_wrsB0 = p.dma("gpsimd", lambda e: e.dma_start(
        out=wrsB[:], in_=s_ws[None, :].broadcast_to((128, C))), deps=[t_wssave])
    t_wrsB = p.step("vector", lambda e: e.reciprocal(out=wrsB[:], in_=wrsB[:]),
                    deps=[t_wrsB0])

    def w_round(e):
        insts = []
        for ct in range(NCT):
            sl = slice(ct * C, (ct + 1) * C)
            insts.append(e.tensor_tensor(out=wTf[:, sl], in0=wTf[:, sl], in1=wrsB[:], op=ALU.mult))
        insts.append(e.tensor_copy(out=wq_i[:], in_=wTf[:]))
        insts.append(e.tensor_copy(out=wqT_s[:], in_=wq_i[:]))
        return insts
    t_wq = p.step("vector", w_round, deps=[t_wrsB, t_ffnld])
    es_p.close()

    es_a = ExitStack()
    def sba(name, shape, dt):
        return es_a.enter_context(nc.sbuf_tensor(name, shape, dt))
    yT = sba("yT", [128, NCT * T], BF16)
    onehot16 = sba("onehot16", [128, 16 * 64], F32)
    t_onehot = p.dma("sync", lambda e: e.dma_start(
        out=onehot16[0:16, :], in_=onehot16_in[:, :]),
        deps=[t_wq, t_ropeb, t_wqk])
    es_a2 = ExitStack()
    def sba2(name, shape, dt):
        return es_a2.enter_context(nc.sbuf_tensor(name, shape, dt))
    t_ln1 = []
    t_n_hist = []
    for tt in range(NT):
        par = tt % 2
        tk_ld = p.dma("sync", lambda e, tt=tt, par=par: e.dma_start(
            out=xtile2[par][:], in_=x_in[tt*128:(tt+1)*128, :]),
            deps=[t_n_hist[-2] if len(t_n_hist) >= 2 else None])

        def norm(e, xt, nm, rs, par=par):
            return e.tensor_scalar(out=sqscr2[par][:], in0=xt[:],
                                   scalar1=nm, scalar2=rs,
                                   op0=ALU.add, op1=ALU.mult)
        tk_n = ln_tile(tk_ld, norm, par=par,
                       extra_deps=[t_ln1[-2] if len(t_ln1) >= 2 else None])
        t_n_hist.append(tk_n)
        tk_st = p.dma("gpsimd", lambda e, tt=tt, par=par: e.dma_start(
            out=s_xn1[tt*128:(tt+1)*128, :], in_=sqscr2[par][:]), deps=[tk_n])
        t_ln1.append(tk_st)

    if "xn1" in dbg_t:
        o = dbg_t["xn1"]
        p.dma("gpsimd", lambda e, o=o: e.dma_start(out=o[:, :], in_=s_xn1[:, :]),
              deps=[t_ln1[-1]])

    xn1T = sba2("xn1T", [128, NCT * T], BF16)
    t_xn1T = p.dma("sync", lambda e: [
        e.dma_start(out=xn1T[:, b*T:(b+1)*T], in_=s_xn1[:, b*128:(b+1)*128],
                    transpose=True) for b in range(NCT)], deps=t_ln1, n=NCT)

    gw_buf = sba2("gw_buf", [128, NCT * 256], BF16)
    qkvslab = {pth: sba2(f"slab_{pth}", [128, 2 * T], BF16) for pth in ("q", "k", "v")}
    qlat_tok = sba2("qlat_tok", [128, NT * 128], BF16)
    klat_tok = sba2("klat_tok", [128, NT * 128], BF16)
    ropetmpb = sba2("ropetmpb", [128, NT * 128], BF16)
    qrT = sba2("qrT", [128, T], BF16)
    qrT2 = sba2("qrT2", [128, T], BF16)
    krT = sba2("krT", [128, T], BF16)
    krT2 = sba2("krT2", [128, T], BF16)
    qfT = sba2("qfT", [128, T], BF16)
    qfT2 = sba2("qfT2", [128, T], BF16)
    votok = sba2("votok", [128, NT * 264], BF16)
    pt_buf = [sba2(f"pt{i}", [128, 512], BF16) for i in range(8)]
    denstage = sba2("denstage", [128, 512], F32)
    densb = sba2("densb", [128, 512], F32)
    ropescr = qlat_tok
    ropescr2 = klat_tok

    srcT = cattnT.rearrange("(ct p) o -> p ct o", p=128)
    t_grp_done = []
    prev_votok_user = None
    prev_slab_user = {"q": None, "k": None, "v": None}
    prev_rope_store = None
    sc_guard = [None] * 4
    pv_guard = [None] * 4

    for g in range(4):

        es_g1 = ExitStack()
        ps_qkv = [es_g1.enter_context(nc.psum_tensor(f"ps_qkv{g}_{i}", [128, 512], F32))
                  for i in range(2)]
        ps_lat = es_g1.enter_context(nc.psum_tensor(f"ps_lat{g}", [128, 128], F32))
        ps_wqk = [es_g1.enter_context(nc.psum_tensor(f"ps_wqk{g}_{i}", [128, 512], F32))
                  for i in range(2)]
        ps_vog = es_g1.enter_context(nc.psum_tensor(f"ps_vog{g}", [128, 128], F32))

        grp_entry = [t_grp_done[-1] if t_grp_done else None, t_wvo, t_vob]

        t_slab = {}
        prev_path_mm = None
        for pth in ("q", "k", "v"):
            base_off = {"q": 0, "k": C, "v": 2 * C}[pth] + g * 256
            gw_deps = [t_grp_done[-1] if t_grp_done else None, t_wq, t_wamax,
                       prev_path_mm]
            if g > 0:
                gw_deps += [prev_slab_user["q"], prev_slab_user["v"]]
            t_gw = p.dma("gpsimd", lambda e, base_off=base_off: e.dma_start(
                out=gw_buf[:], in_=srcT[:, :, base_off:base_off+256]), deps=gw_deps)
            toks = []
            for half in range(2):
                for w in range(4):
                    def mm_qkv(e, pth=pth, half=half, w=w):
                        insts = []
                        for k in range(NCT):
                            lhs = gw_buf[:, k*256 + half*128: k*256 + half*128 + 128]
                            rhs = xn1T[:, k*T + w*512: k*T + (w+1)*512]
                            insts.append(e.matmul(ps_qkv[w % 2][:, :], lhsT=lhs, rhs=rhs,
                                                  start=(k == 0), stop=(k == NCT - 1)))
                        return insts
                    deps = [t_gw, t_xn1T, prev_slab_user[pth]] + grp_entry
                    if len(toks) >= 2:
                        deps.append(toks[-2])
                    tk_mm = p.step("tensor", mm_qkv, deps=deps)
                    prev_path_mm = tk_mm

                    def ev_qkv(e, pth=pth, half=half, w=w, g=g):
                        otile = {"q": 0, "k": 8, "v": 16}[pth] + 2 * g + half
                        return e.activation(
                            out=qkvslab[pth][:, half*T + w*512: half*T + (w+1)*512],
                            in_=ps_qkv[w % 2][:, :], func=AF.Identity,
                            bias=cattnb_c[:, otile:otile+1])
                    tk_ev = p.step("scalar", ev_qkv, deps=[tk_mm, tok_cb])
                    toks.append(tk_ev)
            t_slab[pth] = toks

        lat_evs = []
        for pth, blk, tokbuf in (("q", q2lblk_s, qlat_tok), ("k", kv2lblk_s, klat_tok)):
            for tt in range(NT):
                def mm_lat(e, pth=pth, blk=blk, tt=tt):
                    return [
                        e.matmul(ps_lat[:, 0:64], lhsT=qkvslab[pth][:, tt*128:(tt+1)*128],
                                 rhs=blk[:, :], start=True, stop=True),
                        e.matmul(ps_lat[:, 64:128], lhsT=qkvslab[pth][:, T + tt*128: T + (tt+1)*128],
                                 rhs=blk[:, :], start=True, stop=True),
                    ]
                deps = [t_slab[pth][-1], t_consts, lat_evs[-1] if lat_evs else None]
                tk_mm = p.step("tensor", mm_lat, deps=deps)

                def ev_lat(e, tokbuf=tokbuf, tt=tt):
                    return e.tensor_copy(out=tokbuf[:, tt*128:(tt+1)*128], in_=ps_lat[:, :])
                lat_evs.append(p.step("vector", ev_lat, deps=[tk_mm, prev_rope_store]))

        def do_rope(e):
            insts = []
            for tokbuf, ropeb, dst in ((qlat_tok, ropebq, ropescr), (klat_tok, ropebk, ropescr2)):
                c4 = cosT.rearrange("p (tt l) -> p tt l", tt=NT)[:, :, None, :] \
                         .broadcast_to((128, NT, 4, LAT))
                s4 = ssT.rearrange("p (tt l) -> p tt l", tt=NT)[:, :, None, :] \
                        .broadcast_to((128, NT, 4, LAT))
                rb4 = ropeb.rearrange("p (tt l) -> p tt l", tt=NT)[:, :, None, :] \
                           .broadcast_to((128, NT, 4, LAT))
                v4 = tokbuf.rearrange("p (tt hh l) -> p tt hh l", tt=NT, hh=4)
                d4 = dst.rearrange("p (tt hh l) -> p tt hh l", tt=NT, hh=4)
                t4 = ropetmpb.rearrange("p (tt hh l) -> p tt hh l", tt=NT, hh=4)
                vsw = tokbuf.rearrange("p (blk two) -> p blk two", two=2)
                tsw = ropetmpb.rearrange("p (blk two) -> p blk two", two=2)
                insts.append(e.tensor_copy(out=tsw[:, :, 0:1], in_=vsw[:, :, 1:2]))
                insts.append(e.tensor_copy(out=tsw[:, :, 1:2], in_=vsw[:, :, 0:1]))
                insts.append(e.tensor_tensor(out=t4[:], in0=t4[:], in1=s4, op=ALU.mult))
                insts.append(e.tensor_tensor(out=d4[:], in0=v4[:], in1=c4, op=ALU.mult))
                insts.append(e.tensor_tensor(out=dst[:], in0=dst[:], in1=ropetmpb[:], op=ALU.add))
                insts.append(e.tensor_tensor(out=d4[:], in0=d4[:], in1=rb4, op=ALU.add))
            return insts
        tk_rope = p.step("vector", do_rope,
                         deps=[lat_evs[-1], t_ropeb, t_consts, prev_rope_store])

        tk_strope = p.dma("gpsimd", lambda e: [
            e.dma_start(out=s_qr.rearrange("(tt p) l -> p tt l", p=128),
                        in_=ropescr.rearrange("p (tt l) -> p tt l", tt=NT)),
            e.dma_start(out=s_kr.rearrange("(tt p) l -> p tt l", p=128),
                        in_=ropescr2.rearrange("p (tt l) -> p tt l", tt=NT)),
        ], deps=[tk_rope], n=2)
        prev_rope_store = tk_strope

        tk_trT = p.dma("sync", lambda e: [
            e.dma_start(out=qrT[:, :], in_=s_qr[:, :], transpose=True),
            e.dma_start(out=krT[:, :], in_=s_kr[:, :], transpose=True),
        ], deps=[tk_strope, t_grp_done[-1] if t_grp_done else None], n=2)

        tk_half = p.step("vector", lambda e: [
            e.tensor_copy(out=qrT2[0:64, :], in_=qrT[64:128, :]),
            e.tensor_copy(out=krT2[0:64, :], in_=krT[64:128, :]),
        ], deps=[tk_trT])

        wqk_evs = []
        for w in range(4):
            def mm_wqk(e, w=w):
                return e.matmul(ps_wqk[w % 2][:, :], lhsT=wqkblk_s[:, :],
                                rhs=qrT[:, w*512:(w+1)*512], start=True, stop=True)
            deps = [tk_trT, t_wqk]
            if len(wqk_evs) >= 2:
                deps.append(wqk_evs[-2])
            tk_mm = p.step("tensor", mm_wqk, deps=deps)

            def ev_wqk(e, w=w):
                return e.activation(out=qfT[:, w*512:(w+1)*512], in_=ps_wqk[w % 2][:, :],
                                    func=AF.Identity, bias=wqkb_c[:, 0:1])
            wqk_evs.append(p.step("scalar", ev_wqk, deps=[tk_mm]))

        tk_qf2 = p.step("vector", lambda e: e.tensor_copy(out=qfT2[0:64, :], in_=qfT[64:128, :]),
                        deps=[wqk_evs[-1]])

        tk_vop = p.step("vector", lambda e: e.memset(votok[:], 1.0),
                        deps=[prev_votok_user, t_slab["v"][-1]])
        vo_evs = []
        for second in range(2):
            for tt in range(NT):
                def mm_votok(e, tt=tt, second=second):
                    return e.matmul(ps_vog[:, :],
                                    lhsT=qkvslab["v"][:, second*T + tt*128: second*T + (tt+1)*128],
                                    rhs=wvoblk[:, :], start=True, stop=True)
                deps = [t_wvo, t_vob, t_slab["v"][-1], tk_vop,
                        vo_evs[-1] if vo_evs else None]
                tk_mm = p.step("tensor", mm_votok, deps=deps)

                def ev_votok(e, tt=tt, second=second):
                    off = tt * 264 + second * 132
                    return [
                        e.tensor_copy(out=votok[:, off: off + 64], in_=ps_vog[:, 0:64]),
                        e.tensor_copy(out=votok[:, off + 66: off + 130], in_=ps_vog[:, 64:128]),
                    ]
                vo_evs.append(p.step("vector", ev_votok, deps=[tk_mm]))
        es_g1.close()

        es_g2 = ExitStack()
        ps_sc = [es_g2.enter_context(nc.psum_tensor(f"ps_sc{g}_{i}", [128, 512], F32))
                 for i in range(4)]
        ps_pv = [es_g2.enter_context(nc.psum_tensor(f"ps_pv{g}_{i}", [128, 512], F32))
                 for i in range(2)]
        stage5_entry = [wqk_evs[-1], lat_evs[-1], vo_evs[-1], tk_half, tk_qf2]

        qf_srcs = [(qfT, 0), (qfT, 32), (qfT2, 0), (qfT2, 32)]
        kr_srcs = [(krT, 0), (krT, 32), (krT2, 0), (krT2, 32)]
        t_norm_last = None
        t_ev_all = []
        t_exp_all = []
        pt_guard = {}
        for hp in range(2):
            for w in range(4):
                heads = [2 * hp, 2 * hp + 1]
                pv_started = {h: False for h in heads}
                t_exp_hist = {h: [] for h in heads}
                t_pv_hist = {h: [] for h in heads}
                pend = []
                n_ki = 4 * w + 4
                for ki in range(n_ki):
                    q0 = 512 * w
                    trim = max(0, 128 * ki - q0)
                    n = 512 - trim
                    diag = (128 * ki >= q0)
                    last_ki = (ki == n_ki - 1)
                    newpend = []
                    for li, h in enumerate(heads):
                        qsrc, qb = qf_srcs[h]
                        ksrc, kb = kr_srcs[h]
                        scb = ps_sc[2 * li + (ki % 2)]

                        def mm_sc(e, ki=ki, q0=q0, trim=trim, n=n, scb=scb,
                                  qsrc=qsrc, qb=qb, ksrc=ksrc, kb=kb):
                            return e.matmul(scb[:, trim:trim+n],
                                            lhsT=ksrc[kb:kb+32, ki*128:(ki+1)*128],
                                            rhs=qsrc[qb:qb+32, q0+trim:q0+512],
                                            start=True, stop=True)
                        eh = t_exp_hist[h]
                        deps = list(stage5_entry)
                        deps.append(eh[-2] if len(eh) >= 2 else sc_guard[h])
                        tk_mm = p.step("tensor", mm_sc, deps=deps)

                        pti = 2 * li + (ki % 2)
                        ptb = pt_buf[pti]

                        def do_exp(e, trim=trim, n=n, ptb=ptb, scb=scb):
                            return e.activation(out=ptb[:, trim:trim+n],
                                                in_=scb[:, trim:trim+n], func=AF.Exp)
                        ph = t_pv_hist[h]
                        dexp = [tk_mm, pt_guard.get(pti)]
                        tk_exp = p.step("scalar", do_exp, deps=dexp)
                        eh.append(tk_exp)
                        t_exp_all.append(tk_exp)

                        tk_in_pv = tk_exp
                        if diag:
                            dcol = 128 * ki - q0
                            def do_mask(e, dcol=dcol, ptb=ptb):
                                return e.tensor_tensor(out=ptb[:, dcol:dcol+128],
                                                       in0=ptb[:, dcol:dcol+128],
                                                       in1=tri[:, :], op=ALU.mult)
                            tk_in_pv = p.step("vector", do_mask, deps=[tk_exp, t_consts])

                        def mm_pv(e, h=h, li=li, ki=ki, trim=trim, n=n, ptb=ptb,
                                  first=(not pv_started[h]), last=last_ki):
                            return e.matmul(ps_pv[li][0:65, trim:trim+n],
                                            lhsT=votok[:, ki*264 + 66*h: ki*264 + 66*h + 65],
                                            rhs=ptb[:, trim:trim+n],
                                            start=first, stop=last)
                        dpv = [tk_in_pv]
                        if not pv_started[h]:
                            dpv.append(pv_guard[h])
                        pv_started[h] = True
                        newpend.append((mm_pv, dpv, h, pti))
                    for fn, dpv, h, pti in pend:
                        tk_pv = p.step("tensor", fn, deps=dpv)
                        t_pv_hist[h].append(tk_pv)
                        pt_guard[pti] = tk_pv
                    pend = newpend
                for fn, dpv, h, pti in pend:
                    tk_pv = p.step("tensor", fn, deps=dpv)
                    t_pv_hist[h].append(tk_pv)
                    pt_guard[pti] = tk_pv

                for li, h in enumerate(heads):
                    head_glob = 4 * g + h
                    ct = head_glob // 2
                    coff = (head_glob % 2) * 64
                    r = 4 * w + h

                    def ev_y(e, li=li, w=w, ct=ct, coff=coff):
                        return [
                            e.tensor_copy(out=yT[coff:coff+64, ct*T + w*512: ct*T + w*512 + 512],
                                          in_=ps_pv[li][0:64, :]),
                            e.tensor_copy(out=denstage[64:65, :], in_=ps_pv[li][64:65, :]),
                        ]
                    tk_evy = p.step("vector", ev_y, deps=[t_pv_hist[h][-1]], flush=True)
                    tk_dst = p.dma("gpsimd", lambda e, r=r: e.dma_start(
                        out=s_den[r:r+1, :], in_=denstage[64:65, :]), deps=[tk_evy])
                    t_ev_all.append(tk_dst)
                    pv_guard[h] = tk_evy
                    sc_guard[h] = t_exp_hist[h][-1]

        tk_dload = p.dma("sync", lambda e: e.dma_start(
            out=densb[0:16, :], in_=s_den[:, :]), deps=t_ev_all[-16:])
        tk_rec = p.step("vector", lambda e: e.reciprocal(
            out=densb[0:16, :], in_=densb[0:16, :]), deps=[tk_dload], flush=True)

        dinv_guard = [None, None]
        for w in range(4):
            for h in range(4):
                head_glob = 4 * g + h
                ct = head_glob // 2
                coff = (head_glob % 2) * 64
                r = 4 * w + h
                par = h % 2

                def bc_dinv(e, par=par, r=r):
                    return e.matmul(ps_pv[par][0:64, :], lhsT=onehot16[0:16, r*64:(r+1)*64],
                                    rhs=densb[0:16, :], start=True, stop=True)
                tk_bc = p.step("tensor", bc_dinv,
                               deps=[tk_rec, t_onehot, dinv_guard[par],
                                     pv_guard[2 * hp], pv_guard[2 * hp + 1],
                                     pv_guard[h]])

                def norm_y(e, par=par, w=w, ct=ct, coff=coff):
                    sl = yT[coff:coff+64, ct*T + w*512: ct*T + w*512 + 512]
                    return [
                        e.scalar_tensor_tensor(out=sl, in0=sl, scalar=1.0,
                                               in1=ps_pv[par][0:64, :],
                                               op0=ALU.mult, op1=ALU.mult),
                        e.tensor_scalar(out=sl, in0=sl,
                                        scalar1=vob_c[0:64, 0:1], scalar2=None, op0=ALU.add),
                    ]
                tk_norm = p.step("vector", norm_y, deps=[tk_bc, t_vob])
                dinv_guard[par] = tk_norm
                sc_guard[h] = tk_norm
                t_norm_last = tk_norm
        es_g2.close()

        t_grp_done.append(t_norm_last)
        prev_votok_user = t_norm_last
        prev_slab_user["q"] = lat_evs[-1]
        prev_slab_user["k"] = lat_evs[-1]
        prev_slab_user["v"] = vo_evs[-1]

    if "yT" in dbg_t:
        o = dbg_t["yT"]
        p.dma("gpsimd", lambda e, o=o: e.dma_start(out=o[:, :], in_=yT[:, :]),
              deps=[t_grp_done[-1]])

    es_a2.close()
    cprojT_s = sba("cprojT_s", [128, NCT * C], BF16)
    t_cpw = p.dma("gpsimd", lambda e: e.dma_start(
        out=cprojT_s[:], in_=cprojT.rearrange("(ct p) o -> p ct o", p=128)),
        deps=[t_grp_done[-1]])

    es_cp = ExitStack()
    ps_cp = [es_cp.enter_context(nc.psum_tensor(f"ps_cp{i}", [128, C], F32))
             for i in range(2)]
    t_cp = []
    t_ep_hist = []
    for tt in range(NT):
        par = tt % 2

        def mm_cp(e, tt=tt, par=par):
            insts = []
            for w in range(2):
                for ct in range(NCT):
                    insts.append(e.matmul(ps_cp[par][:, w*512:(w+1)*512],
                                          lhsT=yT[:, ct*T + tt*128: ct*T + (tt+1)*128],
                                          rhs=cprojT_s[:, ct*C + w*512: ct*C + (w+1)*512],
                                          start=(ct == 0), stop=(ct == NCT - 1)))
            return insts
        deps = [t_grp_done[-1], t_cpw,
                t_ep_hist[-2] if len(t_ep_hist) >= 2 else None]
        tk_mm = p.step("tensor", mm_cp, deps=deps)

        tk_ldx = p.dma("scalar", lambda e, tt=tt, par=par: e.dma_start(
            out=xtile2[par][:], in_=x_in[tt*128:(tt+1)*128, :]),
            deps=[t_ln1[-1], t_ep_hist[-2] if len(t_ep_hist) >= 2 else None])

        def ep_cp(e, tt=tt, par=par):
            return [
                e.tensor_tensor(out=sqscr2[par][:], in0=ps_cp[par][:, :], in1=cprojbB[:], op=ALU.add),
                e.tensor_tensor(out=sqscr2[par][:], in0=sqscr2[par][:], in1=xtile2[par][:], op=ALU.add),
            ]
        tk_ep = p.step("vector", ep_cp,
                       deps=[tk_mm, tk_ldx, t_consts2,
                             t_cp[-2] if len(t_cp) >= 2 else None])
        t_ep_hist.append(tk_ep)

        tk_st = p.dma("gpsimd", lambda e, tt=tt, par=par: e.dma_start(
            out=s_xres[tt*128:(tt+1)*128, :], in_=sqscr2[par][:]), deps=[tk_ep])
        t_cp.append(tk_st)
    es_cp.close()
    es_a.close()

    if "x1" in dbg_t:
        o = dbg_t["x1"]
        p.dma("gpsimd", lambda e, o=o: e.dma_start(out=o[:, :], in_=s_xres[:, :]),
              deps=[t_cp[-1]])

    xs_c = sb("xs_c", [128, NT], F32)

    _ffn_ctr = [0]

    def ffn_phase(src, dst, t_prev):
        tag = _ffn_ctr[0]; _ffn_ctr[0] += 1
        es_ff = ExitStack()
        ps_f = [es_ff.enter_context(nc.psum_tensor(f"ps_f{tag}_{i}", [128, C], F32))
                for i in range(2)]
        xq_tok = es_ff.enter_context(nc.sbuf_tensor(f"xq_tok{tag}", [128, 2 * C], BF16))
        xq_i = es_ff.enter_context(nc.sbuf_tensor(f"xq_i{tag}", [128, C], I32))
        xqT = es_ff.enter_context(nc.sbuf_tensor(f"xqT{tag}", [128, NCT * T], BF16))
        f_scr = [es_ff.enter_context(nc.sbuf_tensor(f"f_scr{tag}_{i}", [128, C], F32))
                 for i in range(2)]
        t_q = []
        t_qn = []
        tk_q_prev = None
        for tt in range(NT):
            par = tt % 2
            tk_ld = p.dma("sync", lambda e, tt=tt, par=par: e.dma_start(
                out=xtile2[par][:], in_=src[tt*128:(tt+1)*128, :]),
                deps=[t_prev, t_q[-2] if len(t_q) >= 2 else None])

            def norm(e, xt, nm, rs, par=par):
                return e.tensor_scalar(out=sqscr2[par][:], in0=xt[:],
                                       scalar1=nm, scalar2=rs,
                                       op0=ALU.add, op1=ALU.mult)
            tk_n = ln_tile(tk_ld, norm, par=par,
                           extra_deps=[t_q[-2] if len(t_q) >= 2 else None])

            amax = rowsc[par][:, 5:6]
            rs_q = rowsc[par][:, 6:7]
            tk_q1 = p.step("vector", lambda e, par=par, amax=amax: e.reduce_max(
                out=amax, in_=sqscr2[par][:], axis=AX.X, apply_absolute_value=True),
                deps=[tk_n])
            tk_q2 = p.step("gpsimd", lambda e, amax=amax: e.tensor_scalar(
                out=amax, in0=amax, scalar1=EPS_Q, scalar2=None, op0=ALU.max),
                deps=[tk_q1])
            tk_q3v = p.step("vector", lambda e, amax=amax, rs_q=rs_q: e.reciprocal(
                out=rs_q, in_=amax), deps=[tk_q2], flush=True)
            tk_q3s = p.step("scalar", lambda e, tt=tt, amax=amax: e.activation(
                out=xs_c[:, tt:tt+1], in_=amax, func=AF.Identity, scale=1.0 / QMAX),
                deps=[tk_q2])
            def quant(e, tt=tt, par=par, rs_q=rs_q):
                return [
                    e.tensor_scalar(out=sqscr2[par][:], in0=sqscr2[par][:], scalar1=rs_q,
                                    scalar2=QMAX, op0=ALU.mult, op1=ALU.mult),
                    e.tensor_copy(out=xq_i[:], in_=sqscr2[par][:]),
                    e.tensor_copy(out=xq_tok[:, (tt % 2)*C:(tt % 2 + 1)*C], in_=xq_i[:]),
                ]
            tk_q = p.step("vector", quant,
                          deps=[tk_q3v, tk_q3s, t_q[-2] if len(t_q) >= 2 else None])
            tk_q_prev = tk_q

            tk_st = p.dma("gpsimd", lambda e, tt=tt: e.dma_start(
                out=s_xq[tt*128:(tt+1)*128, :], in_=xq_tok[:, (tt % 2)*C:(tt % 2 + 1)*C]),
                deps=[tk_q])
            t_q.append(tk_st)
            t_qn.append(tk_q)

        t_xqT = p.dma("sync", lambda e: [
            e.dma_start(out=xqT[:, b*T:(b+1)*T], in_=s_xq[:, b*128:(b+1)*128],
                        transpose=True) for b in range(NCT)], deps=t_q, n=NCT)

        t_f = []
        t_fep = []
        for tt in range(NT):
            par = tt % 2

            def mm_f(e, tt=tt, par=par):
                insts = []
                for w in range(2):
                    for ct in range(NCT):
                        insts.append(e.matmul(ps_f[par][:, w*512:(w+1)*512],
                                              lhsT=xqT[:, ct*T + tt*128: ct*T + (tt+1)*128],
                                              rhs=wqT_s[:, ct*C + w*512: ct*C + (w+1)*512],
                                              start=(ct == 0), stop=(ct == NCT - 1)))
                return insts
            deps = [t_xqT, t_wq, t_fep[-2] if len(t_fep) >= 2 else None]
            tk_mm = p.step("tensor", mm_f, deps=deps)

            tk_ldx = p.dma("scalar", lambda e, tt=tt, par=par: e.dma_start(
                out=xtile2[par][:], in_=src[tt*128:(tt+1)*128, :]),
                deps=[t_qn[-1], t_fep[-2] if len(t_fep) >= 2 else None])

            def ep(e, tt=tt, par=par):
                return [
                    e.tensor_scalar(out=f_scr[par][:], in0=ps_f[par][:, :], scalar1=xs_c[:, tt:tt+1],
                                    scalar2=None, op0=ALU.mult),
                    e.tensor_tensor(out=f_scr[par][:], in0=f_scr[par][:], in1=wsB[:], op=ALU.mult),
                    e.tensor_tensor(out=f_scr[par][:], in0=f_scr[par][:], in1=ffnbB[:], op=ALU.add),
                    e.tensor_tensor(out=f_scr[par][:], in0=f_scr[par][:], in1=xtile2[par][:], op=ALU.add),
                ]
            tk_ep = p.step("vector", ep, deps=[tk_mm, tk_ldx, t_wsB, t_consts2,
                                               t_f[-2] if len(t_f) >= 2 else None])
            t_fep.append(tk_ep)

            tk_st = p.dma("gpsimd", lambda e, tt=tt, par=par: e.dma_start(
                out=dst[tt*128:(tt+1)*128, :], in_=f_scr[par][:]), deps=[tk_ep])
            t_f.append(tk_st)
        es_ff.close()
        return t_f[-1]

    t_ffn1 = ffn_phase(s_xres, s_xres, t_cp[-1])

    if "x2" in dbg_t:
        o = dbg_t["x2"]
        p.dma("gpsimd", lambda e, o=o: e.dma_start(out=o[:, :], in_=s_xres[:, :]),
              deps=[t_ffn1])

    es_sm = ExitStack()
    def sbs(name, shape, dt):
        return es_sm.enter_context(nc.sbuf_tensor(name, shape, dt))
    _x3t = sbs("xn3tok", [128, C], BF16)
    xn3tok = [_x3t, _x3t]
    t_n3 = []
    t_n3n = []
    for tt in range(NT):
        par = tt % 2
        tk_ld = p.dma("sync", lambda e, tt=tt, par=par: e.dma_start(
            out=xtile2[par][:], in_=s_xres[tt*128:(tt+1)*128, :]),
            deps=[t_ffn1, t_n3n[-2] if len(t_n3n) >= 2 else None])

        def norm(e, xt, nm, rs, par=par):
            return e.tensor_scalar(out=xn3tok[par][:], in0=xt[:],
                                   scalar1=nm, scalar2=rs,
                                   op0=ALU.add, op1=ALU.mult)
        tk_n = ln_tile(tk_ld, norm, par=par,
                       extra_deps=[t_n3[-1] if t_n3 else None])
        t_n3n.append(tk_n)
        tk_st = p.dma("gpsimd", lambda e, tt=tt, par=par: e.dma_start(
            out=s_xn3[tt*128:(tt+1)*128, :], in_=xn3tok[par][:]), deps=[tk_n])
        t_n3.append(tk_st)

    xn3T = sbs("xn3T", [128, NCT * BAND], BF16)
    tk_padz = p.step("vector", lambda e: e.memset(xn3T[:], 0.0))
    t_xn3T = p.dma("sync", lambda e: [
        e.dma_start(out=xn3T[:, b*BAND + CONV_PAD: (b+1)*BAND],
                    in_=s_xn3[:, b*128:(b+1)*128], transpose=True)
        for b in range(NCT)], deps=t_n3 + [tk_padz], n=NCT)

    convw_c = sbs("convw_c", [128, NCT * 4], F32)
    convb_c = sbs("convb_c", [128, NCT], F32)
    dtw_c = sbs("dtw_c", [128, NCT], F32)
    dtb_c = sbs("dtb_c", [128, NCT], F32)
    dp_c = sbs("dp_c", [128, NCT], F32)
    acol = sbs("acol", [128, NCT * NS], F32)
    es_s2 = ExitStack()
    xc = es_s2.enter_context(nc.sbuf_tensor("xc", [128, NCT * T], BF16))
    xprojT_s = es_s2.enter_context(nc.sbuf_tensor("xprojT_s", [128, NCT * 65], BF16))
    convacc = es_s2.enter_context(nc.sbuf_tensor("convacc", [128, T], F32))
    bcsb = es_s2.enter_context(nc.sbuf_tensor("bcsb", [128, T], F32))
    bc16 = es_s2.enter_context(nc.sbuf_tensor("bc16", [128, T], BF16))

    t_ssmld = p.dma("gpsimd", lambda e: [
        e.dma_start(out=convw_c[:], in_=conv_w[:, :]),
        e.dma_start(out=convb_c[:], in_=convb_pc[:, :]),
        e.dma_start(out=dtw_c[:], in_=dtw_pc[:, :]),
        e.dma_start(out=dtb_c[:], in_=dtb_pc[:, :]),
        e.dma_start(out=dp_c[:], in_=dp_pc[:, :]),
        e.dma_start(out=acol[:], in_=a_log.rearrange("(t p) n -> p t n", p=128)),
    ], n=6, deps=[t_ffn1])
    t_alog1 = p.step("scalar", lambda e: e.activation(out=acol[:], in_=acol[:], func=AF.Exp),
                     deps=[t_ssmld])
    t_alog = p.step("vector", lambda e: e.tensor_scalar(
        out=acol[:], in0=acol[:], scalar1=-1.0, scalar2=None, op0=ALU.mult), deps=[t_alog1])

    t_xc = []
    t_xcst = []
    for b in range(NCT):
        def conv_f(e, b=b):
            base = b * BAND + CONV_PAD
            insts = [e.tensor_scalar(out=convacc[:], in0=xn3T[:, base-3:base-3+T],
                                     scalar1=convw_c[:, 4*b:4*b+1], scalar2=None, op0=ALU.mult)]
            for j in range(1, 4):
                insts.append(e.scalar_tensor_tensor(out=convacc[:],
                                                    in0=xn3T[:, base-3+j:base-3+j+T],
                                                    scalar=convw_c[:, 4*b+j:4*b+j+1],
                                                    in1=convacc[:], op0=ALU.mult, op1=ALU.add))
            return insts
        tk_c = p.step("vector", conv_f, deps=[t_xn3T, t_ssmld, t_xc[-1] if t_xc else None])
        tk_s = p.step("scalar", lambda e, b=b: e.activation(
            out=xc[:, b*T:(b+1)*T], in_=convacc[:], func=AF.Silu, bias=convb_c[:, b:b+1]),
            deps=[tk_c])
        tk_xst = p.dma("gpsimd", lambda e, b=b: e.dma_start(
            out=s_xc[b*128:(b+1)*128, :], in_=xc[:, b*T:(b+1)*T]), deps=[tk_s])
        t_xc.append(tk_s)
        t_xcst.append(tk_xst)

    if "xc" in dbg_t:
        o = dbg_t["xc"]
        p.dma("gpsimd", lambda e, o=o: e.dma_start(out=o[:, :], in_=xc[:, :]), deps=[t_xc[-1]])

    t_xp = p.dma("gpsimd", lambda e: e.dma_start(
        out=xprojT_s[:], in_=xprojT_pad.rearrange("(ct p) m -> p ct m", p=128)),
        deps=[t_ffn1])

    es_s = ExitStack()
    ps_xp = es_s.enter_context(nc.psum_tensor("ps_xp", [128, 512], F32))
    dtrawB = es_s.enter_context(nc.psum_tensor("ps_dtb4", [128, T], F32))
    t_xpj = []
    for w in range(4):
        def mm_xp(e, w=w):
            insts = []
            for ct in range(NCT):
                insts.append(e.matmul(ps_xp[0:65, :], lhsT=xprojT_s[:, ct*65:(ct+1)*65],
                                      rhs=xc[:, ct*T + w*512: ct*T + (w+1)*512],
                                      start=(ct == 0), stop=(ct == NCT - 1)))
            return insts
        tk_mm = p.step("tensor", mm_xp, deps=[t_xc[-1], t_xp, t_xpj[-1] if t_xpj else None])
        t_xpj.append(p.step("vector", lambda e, w=w: e.tensor_copy(
            out=bcsb[0:65, w*512:(w+1)*512], in_=ps_xp[0:65, :]), deps=[tk_mm]))

    if "proj" in dbg_t:
        o = dbg_t["proj"]
        p.dma("gpsimd", lambda e, o=o: e.dma_start(out=o[:, :], in_=bcsb[:, :]), deps=[t_xpj[-1]])

    t_bc16 = p.step("vector", lambda e: [
        e.tensor_copy(out=bc16[0:16, :], in_=bcsb[0:16, :]),
        e.tensor_copy(out=bc16[32:48, :], in_=bcsb[32:48, :]),
    ], deps=[t_xpj[-1]])
    t_bcst = p.dma("gpsimd", lambda e: [
        e.dma_start(out=s_bc[0:16, :], in_=bc16[0:16, :]),
        e.dma_start(out=s_bcf[0:16, :], in_=bcsb[32:48, :]),
    ], deps=[t_bc16], n=2)

    t_dtb_l = []
    for w in range(4):
        tk_mm = p.step("tensor", lambda e, w=w: e.matmul(
            dtrawB[:, w*512:(w+1)*512], lhsT=ones_f[64:65, 0:128],
            rhs=bcsb[64:65, w*512:(w+1)*512], start=True, stop=True),
            deps=[t_xpj[-1], t_ones, t_dtb_l[-1] if t_dtb_l else None])
        t_dtb_l.append(tk_mm)

    es_s2.close()
    NSL = 4
    BPS = NCT // NSL
    delta_sl = sbs("delta_sl", [128, BPS * T], F32)
    u_sl = sbs("u_sl", [128, BPS * T], BF16)
    xcsl = [sbs(f"xcsl{i}", [128, BPS * T], BF16) for i in range(2)]
    zscr = sbs("zscr", [128, T], F32)
    spscr = sbs("spscr", [128, T], F32)
    abar = [zscr, spscr]
    ubuf = [sbs("ubuf0", [128, T], BF16), sbs("ubuf1", [128, T], BF16)]
    hbuf = sbs("hbuf", [128, T], F32)
    yacc = sbs("yacc", [128, BPS * T], F32)
    bB = [sbs("bB0", [128, T], BF16), sbs("bB1", [128, T], BF16)]
    cB = [sbs("cB0", [128, T], F32), sbs("cB1", [128, T], F32)]
    yhi = bB[0]

    t_sl_done = None
    t_xcl = {}
    t_du_prev = {}

    def emit_xcl(sl):
        if sl >= NSL or sl in t_xcl:
            return
        t_xcl[sl] = p.dma("sync", lambda e, sl=sl: e.dma_start(
            out=xcsl[sl % 2][:],
            in_=s_xc.rearrange("(bb pp) t -> pp bb t", pp=128)[:, sl*BPS:(sl+1)*BPS, :]),
            deps=[t_xcst[-1], t_xpj[-1], t_dtb_l[-1], t_bcst,
                  t_du_prev.get(sl - 2)])

    emit_xcl(0)
    emit_xcl(1)
    for sl in range(NSL):
        bands = [sl * BPS + i for i in range(BPS)]
        t_du = []
        for i, b in enumerate(bands):
            def mk_z(e, i=i, b=b):
                return e.tensor_scalar(out=delta_sl[:, i*T:(i+1)*T], in0=dtrawB[:],
                                       scalar1=dtw_c[:, b:b+1], scalar2=dtb_c[:, b:b+1],
                                       op0=ALU.mult, op1=ALU.add)
            tk_z = p.step("vector", mk_z, deps=[t_dtb_l[-1], t_ssmld, t_sl_done, t_bcst,
                                                t_xcst[-1], t_xpj[-1]])

            def sp1(e, i=i):
                return [
                    e.activation(out=zscr[:], in_=delta_sl[:, i*T:(i+1)*T], func=AF.Abs),
                    e.activation(out=zscr[:], in_=zscr[:], func=AF.Exp, scale=-1.0),
                    e.activation(out=zscr[:], in_=zscr[:], func=AF.Ln, bias=1.0),
                ]
            tk_sp1 = p.step("scalar", sp1, deps=[tk_z, t_cpool])

            def sp2(e, i=i, b=b, sl=sl):
                return [
                    e.tensor_scalar(out=spscr[:], in0=delta_sl[:, i*T:(i+1)*T],
                                    scalar1=0.0, scalar2=None, op0=ALU.max),
                    e.tensor_tensor(out=delta_sl[:, i*T:(i+1)*T], in0=spscr[:], in1=zscr[:], op=ALU.add),
                    e.scalar_tensor_tensor(out=u_sl[:, i*T:(i+1)*T], in0=delta_sl[:, i*T:(i+1)*T],
                                           scalar=1.0, in1=xcsl[sl % 2][:, i*T:(i+1)*T],
                                           op0=ALU.mult, op1=ALU.mult),
                    e.memset(yacc[:, i*T:(i+1)*T], 0.0),
                ]
            t_du.append(p.step("vector", sp2, deps=[tk_sp1, t_xcl[sl]]))

        if "delta" in dbg_t and sl == 0:
            o = dbg_t["delta"]
            p.dma("gpsimd", lambda e, o=o: e.dma_start(out=o[:, :], in_=delta_sl[:, :]),
                  deps=[t_du[-1]])

        t_nprev = None
        t_scan_hist = []
        t_y_hist = []
        for n_ in range(NS):
            tk_bc = p.dma("scalar", lambda e, n_=n_, pb=n_ % 2: [
                e.dma_start(out=bB[pb][:], in_=s_bc[n_:n_+1, :].broadcast_to((128, T))),
                e.dma_start(out=cB[pb][:], in_=s_bcf[n_:n_+1, :].broadcast_to((128, T))),
            ], deps=[t_bcst, t_dtb_l[-1], t_y_hist[-3] if len(t_y_hist) >= 3 else None,
                     t_sl_done if n_ == 0 else None], n=2)

            for i, b in enumerate(bands):
                j = n_ * BPS + i
                ab_buf = abar[j % 2]
                ub_buf = ubuf[j % 2]

                def mk_abar(e, i=i, b=b, n_=n_, ab_buf=ab_buf):
                    return e.activation(out=ab_buf[:], in_=delta_sl[:, i*T:(i+1)*T],
                                        func=AF.Exp, scale=acol[:, b*NS + n_: b*NS + n_ + 1])
                tk_ab = p.step("scalar", mk_abar,
                               deps=[t_du[-1], t_alog,
                                     t_scan_hist[-2] if len(t_scan_hist) >= 2 else None])

                def mk_ub(e, i=i, n_=n_, ub_buf=ub_buf):
                    return e.tensor_tensor(out=ub_buf[:], in0=u_sl[:, i*T:(i+1)*T],
                                           in1=bB[n_ % 2][:], op=ALU.mult)
                tk_ub = p.step("vector", mk_ub,
                               deps=[tk_bc, t_du[-1],
                                     t_scan_hist[-2] if len(t_scan_hist) >= 2 else None])

                tk_scan = p.step("vector", lambda e, ab_buf=ab_buf, ub_buf=ub_buf:
                                 e.tensor_tensor_scan(
                                     out=hbuf[:], data0=ab_buf[:], data1=ub_buf[:],
                                     initial=0.0, op0=ALU.mult, op1=ALU.add),
                                 deps=[tk_ab, tk_ub,
                                       t_y_hist[-1] if t_y_hist else None])
                t_scan_hist.append(tk_scan)

                tk_ym = p.step("vector", lambda e, n_=n_: e.tensor_tensor(
                    out=hbuf[:], in0=hbuf[:], in1=cB[n_ % 2][:], op=ALU.mult),
                    deps=[tk_scan, tk_bc])
                t_nprev = p.step("vector", lambda e, i=i: e.tensor_tensor(
                    out=yacc[:, i*T:(i+1)*T], in0=yacc[:, i*T:(i+1)*T],
                    in1=hbuf[:], op=ALU.add),
                    deps=[tk_ym])
                t_y_hist.append(t_nprev)

        if "ussm" in dbg_t and sl == 0:
            o = dbg_t["ussm"]
            p.dma("gpsimd", lambda e, o=o: e.dma_start(out=o[:, :], in_=u_sl[:, :]),
                  deps=[t_du[-1]])
        t_du_prev[sl] = t_du[-1]
        emit_xcl(sl + 2)
        t_bandst = []
        for i, b in enumerate(bands):
            ylo = ubuf[0]
            def fin_y(e, i=i, b=b, ylo=ylo):
                base = b * BAND + CONV_PAD
                return [
                    e.scalar_tensor_tensor(out=yacc[:, i*T:(i+1)*T],
                                           in0=xn3T[:, base:base+T], scalar=dp_c[:, b:b+1],
                                           in1=yacc[:, i*T:(i+1)*T], op0=ALU.mult, op1=ALU.add),
                    e.tensor_copy(out=yhi[:], in_=yacc[:, i*T:(i+1)*T]),
                    e.tensor_copy(out=zscr[:], in_=yhi[:]),
                    e.tensor_tensor(out=zscr[:], in0=yacc[:, i*T:(i+1)*T], in1=zscr[:],
                                    op=ALU.subtract),
                    e.tensor_copy(out=ylo[:], in_=zscr[:]),
                ]
            tk_f = p.step("vector", fin_y,
                          deps=[t_nprev, t_ssmld, t_bandst[-1] if t_bandst else None])
            tk_sy = p.dma("gpsimd", lambda e, b=b, ylo=ylo: [
                e.dma_start(out=s_yhi[b*128:(b+1)*128, :], in_=yhi[:]),
                e.dma_start(out=s_ylo[b*128:(b+1)*128, :], in_=ylo[:]),
            ], deps=[tk_f], n=2)
            t_bandst.append(tk_sy)
        t_sl_done = t_bandst[-1]
    es_s.close()
    es_sm.close()

    es_r = ExitStack()
    hiT = [es_r.enter_context(nc.sbuf_tensor(f"hiT{i}", [128, C], BF16)) for i in range(2)]
    loT = [es_r.enter_context(nc.sbuf_tensor(f"loT{i}", [128, C], BF16)) for i in range(2)]
    hiTf = es_r.enter_context(nc.sbuf_tensor("hiTf", [128, C], F32))
    t_r3 = []
    t_r3a = []
    for tt in range(NT):
        par = tt % 2
        tk_tr = p.dma("sync", lambda e, tt=tt, par=par: [
            e.dma_start(out=hiT[par][:], in_=s_yhi[:, tt*128:(tt+1)*128], transpose=True),
            e.dma_start(out=loT[par][:], in_=s_ylo[:, tt*128:(tt+1)*128], transpose=True),
        ], deps=[t_sl_done, t_r3a[-2] if len(t_r3a) >= 2 else None], n=2)

        tk_ld = p.dma("scalar", lambda e, tt=tt, par=par: e.dma_start(
            out=xtile2[par][:], in_=s_xres[tt*128:(tt+1)*128, :]),
            deps=[t_n3[-1], t_r3[-2] if len(t_r3) >= 2 else None])

        def add_y(e, tt=tt, par=par):
            return [
                e.tensor_copy(out=hiTf[:], in_=hiT[par][:]),
                e.tensor_tensor(out=xtile2[par][:], in0=xtile2[par][:], in1=hiTf[:], op=ALU.add),
                e.tensor_copy(out=hiTf[:], in_=loT[par][:]),
                e.tensor_tensor(out=xtile2[par][:], in0=xtile2[par][:], in1=hiTf[:], op=ALU.add),
            ]
        tk_a = p.step("vector", add_y, deps=[tk_tr, tk_ld])
        t_r3a.append(tk_a)
        tk_st = p.dma("gpsimd", lambda e, tt=tt, par=par: e.dma_start(
            out=s_xres[tt*128:(tt+1)*128, :], in_=xtile2[par][:]), deps=[tk_a])
        t_r3.append(tk_st)

    if "yhi" in dbg_t:
        o1 = dbg_t["yhi"]; o2 = dbg_t["ylo"]
        p.dma("gpsimd", lambda e, o=o1: e.dma_start(out=o[:, :], in_=s_yhi[:, :]),
              deps=[t_r3[-1]])
        p.dma("gpsimd", lambda e, o=o2: e.dma_start(out=o[:, :], in_=s_ylo[:, :]),
              deps=[t_r3[-1]])

    if "x3" in dbg_t:
        o = dbg_t["x3"]
        p.dma("gpsimd", lambda e, o=o: e.dma_start(out=o[:, :], in_=s_xres[:, :]),
              deps=[t_r3[-1]])

    es_r.close()
    t_ffn2 = ffn_phase(s_xres, out, t_r3[-1])

    last_out = (p.dsem["gpsimd"], p.dcnt["gpsimd"])

    def wait_all(e):
        e.wait_ge(last_out[0], last_out[1])
        return e.nop()
    p.step("sync", wait_all)

    blk = es.enter_context(nc.Block())
    for name in ENGS:
        ops = p.ops[name]
        getattr(blk, name)(lambda e, ops=ops: [f(e) for f in ops])
    es.close()
    return nc


_NC_CACHE = {}


def _get_nc(debug=()):
    key = tuple(sorted(debug))
    if key not in _NC_CACHE:
        _NC_CACHE[key] = build_nc(debug=key)
    return _NC_CACHE[key]


def _host_consts():
    inv_freq = 1.0 / 10000.0 ** (np.arange(0, LAT, 2, dtype=np.float32) / LAT)
    t = np.arange(T, dtype=np.float32)
    freqs = np.outer(t, inv_freq)
    cos = np.repeat(np.cos(freqs), 2, axis=-1)[:, :LAT].astype(np.float32)
    sin = np.repeat(np.sin(freqs), 2, axis=-1)[:, :LAT].astype(np.float32)
    ss = sin.copy()
    ss[:, 0::2] *= -1.0
    tri = (np.arange(128)[:, None] <= np.arange(128)[None, :]).astype(np.float32)
    return cos, ss, tri


def _blkdiag(m, k):
    r, c = m.shape
    o = np.zeros((r * k, c * k), dtype=m.dtype)
    for i in range(k):
        o[i*r:(i+1)*r, i*c:(i+1)*c] = m
    return o


def _onehot16():
    o = np.zeros((16, 16 * 64), np.float32)
    for r in range(16):
        o[r, r*64:(r+1)*64] = 1.0
    return o


def _swap_pairs(v):
    return np.ascontiguousarray(v.reshape(-1, 2)[:, ::-1].reshape(-1))


def make_in_maps(inputs):
    cos, ss, tri = _host_consts()
    f32 = lambda a: np.ascontiguousarray(np.asarray(a, np.float32))
    aa = lambda k: f32(inputs["L0_attn_a_" + k])
    ab = lambda k: f32(inputs["L0_attn_b_" + k])
    base = {
        "c_attnb_pc": np.ascontiguousarray(aa("c_attn_b").reshape(24, 128).T),
        "c_attnT": f32(aa("c_attn_w").T),
        "c_q2l_blk": _blkdiag(f32(aa("q2l_w").T), 2),
        "c_kv2l_blk": _blkdiag(f32(aa("kv2l_w").T), 2),
        "L0_attn_a_q2l_b": aa("q2l_b"),
        "c_q2l_b_swap": _swap_pairs(aa("q2l_b")),
        "L0_attn_a_kv2l_b": aa("kv2l_b"),
        "c_kv2l_b_swap": _swap_pairs(aa("kv2l_b")),
        "L0_attn_a_kv2l_w": aa("kv2l_w"),
        "c_l2oT": f32(aa("l2o_w").T),
        "L0_attn_a_l2o_b": aa("l2o_b"),
        "c_wqk_blkT": _blkdiag(f32(aa("wqk_w").T), 4),
        "c_wqk_b4": np.tile(aa("wqk_b"), 4),
        "c_cprojT": f32(aa("cproj_w").T),
        "L0_attn_a_cproj_b": aa("cproj_b"),
        "L0_ffn_weight": f32(inputs["L0_ffn_weight"]),
        "c_ffn_wT": f32(f32(inputs["L0_ffn_weight"]).T),
        "L0_ffn_bias": f32(inputs["L0_ffn_bias"]),
        "c_convw": np.ascontiguousarray(
            ab("conv_w")[:, 0, :].reshape(NCT, 128, 4).transpose(1, 0, 2).reshape(128, NCT * 4)),
        "c_convb_pc": np.ascontiguousarray(ab("conv_b").reshape(NCT, 128).T),
        "c_dtw_pc": np.ascontiguousarray(ab("dt_w")[:, 0].reshape(NCT, 128).T),
        "c_dtb_pc": np.ascontiguousarray(ab("dt_b").reshape(NCT, 128).T),
        "L0_attn_b_a_log": ab("a_log"),
        "c_dp_pc": np.ascontiguousarray(ab("Dp").reshape(NCT, 128).T),
        "c_onehot16": _onehot16(),
        "c_cos": cos,
        "c_ss": ss,
        "c_trimask": tri,
    }
    xp = ab("xproj_w")
    xpad = np.zeros((C, 65), np.float32)
    xpad[:, 0:16] = xp[0:16].T
    xpad[:, 32:48] = xp[16:32].T
    xpad[:, 64] = xp[32]
    base["c_xprojT_pad"] = xpad

    x_full = f32(inputs["L0_attn_a_x"])
    return [dict(base, x=np.ascontiguousarray(x_full[c])) for c in range(8)]


def kernel(**inputs):
    nc = _get_nc()
    in_maps = make_in_maps(inputs)
    res = run_bass_kernel_spmd(nc, in_maps, core_ids=list(range(8)))
    return np.stack([np.asarray(r["out"], np.float32) for r in res.results], axis=0)
